# version 03
# baseline (speedup 1.0000x reference)
import numpy as np
import ml_dtypes
from contextlib import ExitStack

import concourse.bass as bass
import concourse.mybir as mybir
from concourse.bass_utils import run_bass_kernel_spmd

F32 = mybir.dt.float32
BF16 = mybir.dt.bfloat16
AF = mybir.ActivationFunctionType
ALU = mybir.AluOpType
AX = mybir.AxisListType

NCORES = 8
T = 4096
D = 1024
NT = T // 128
DC = D // 128
E = 8

BFNP = ml_dtypes.bfloat16

ENGINES = ("sync", "gpsimd", "tensor", "scalar", "vector")



class Step:
    __slots__ = ("idx", "eng", "fn", "reads", "writes", "dma", "deps",
                 "need_inc", "seq", "fam", "memberidx")

    def __init__(self, idx, eng, fn, reads, writes, dma):
        self.idx = idx
        self.eng = eng
        self.fn = fn
        self.reads = reads
        self.writes = writes
        self.dma = dma
        self.deps = []
        self.need_inc = False
        self.seq = 0
        self.fam = None
        self.memberidx = 0


class Sched:

    def __init__(self, nc, ctx, serialize=False):
        self.nc = nc
        self.ctx = ctx
        self.steps = []
        self.last_w = {}
        self.readers = {}
        self.serialize = serialize
        self.sems = {e: ctx.enter_context(nc.semaphore(f"sched_{e}")) for e in ENGINES}
        self.fams = {}

    def _fam(self, key, eng, bulk):
        f = self.fams.get(key)
        if f is None:
            f = {"sem": self.ctx.enter_context(self.nc.semaphore(f"fam_{len(self.fams)}")),
                 "eng": eng, "count": 0, "bulk": bulk, "key": key}
            self.fams[key] = f
        if f["eng"] != eng:
            raise RuntimeError(f"fam {key} used from {eng} and {f['eng']}")
        if bulk and not f["bulk"]:
            raise RuntimeError(f"fam {key} bulk mismatch")
        return f

    def step(self, eng, fn, reads=(), writes=(), dma=False, fam=None, bulk=False):
        i = len(self.steps)
        st = Step(i, eng, fn, tuple(reads), tuple(writes), dma)
        if dma:
            if fam is None:
                w0 = writes[0] if writes else ("anon", i)
                fam = w0[0] if isinstance(w0, tuple) else w0
                fam = f"{eng}:{fam}" + ("" if not isinstance(w0, tuple) or len(w0) < 2
                                        else f":{w0[1]}")
            f = self._fam(fam, eng, bulk)
            f["count"] += 1
            st.fam = fam
            st.memberidx = f["count"]
        deps = set()
        for t in st.reads:
            if t in self.last_w:
                deps.add(self.last_w[t])
        for t in st.writes:
            if t in self.last_w:
                deps.add(self.last_w[t])
            for r in self.readers.get(t, ()):
                deps.add(r)
        if self.serialize and i > 0:
            deps.add(i - 1)
        st.deps = sorted(deps)
        self.steps.append(st)
        for t in st.reads:
            self.readers.setdefault(t, []).append(i)
        for t in st.writes:
            self.last_w[t] = i
            self.readers[t] = []
        return i

    def finalize(self):
        for st in self.steps:
            for d in st.deps:
                p = self.steps[d]
                if not (p.eng == st.eng == "tensor" and not p.dma):
                    p.need_inc = True
        counts = {e: 0 for e in ENGINES}
        for st in self.steps:
            if st.dma:
                st.need_inc = True
                continue
            if st.need_inc:
                counts[st.eng] += 1
                st.seq = counts[st.eng]
        self.final_counts = counts

    def _wait_of(self, p):
        if p.dma:
            f = self.fams[p.fam]
            v = 16 * (f["count"] if f["bulk"] else p.memberidx)
            return (p.fam, f["sem"], v)
        return (p.eng, self.sems[p.eng], p.seq)

    def emit_engine(self, name, eng):
        watermark = {}
        for st in self.steps:
            if st.eng != name:
                continue
            for d in st.deps:
                p = self.steps[d]
                if p.eng == name == "tensor" and not p.dma:
                    continue
                if not p.need_inc:
                    raise RuntimeError("dep without inc")
                key, sem, val = self._wait_of(p)
                if val > watermark.get(key, 0):
                    eng.wait_ge(sem, val)
                    watermark[key] = val
            inst = st.fn(eng)
            if st.need_inc:
                if inst is None:
                    raise RuntimeError(f"step {st.idx} on {name} returned no inst")
                if st.dma:
                    inst.then_inc(self.fams[st.fam]["sem"], 16)
                else:
                    inst.then_inc(self.sems[name], 1)
        for f in self.fams.values():
            if f["eng"] == name and f["count"]:
                eng.wait_ge(f["sem"], 16 * f["count"])

    def build_block(self, ctx):
        self.finalize()
        block = ctx.enter_context(self.nc.Block())

        @block.sync
        def _(e):
            self.emit_engine("sync", e)

        @block.gpsimd
        def _(e):
            self.emit_engine("gpsimd", e)

        @block.tensor
        def _(e):
            self.emit_engine("tensor", e)

        @block.scalar
        def _(e):
            self.emit_engine("scalar", e)

        @block.vector
        def _(e):
            self.emit_engine("vector", e)



def host_consts(core_id):
    ident = np.eye(128, dtype=np.float32)
    ltri = (np.arange(128)[:, None] < np.arange(128)[None, :]).astype(np.float32)
    bmask = np.zeros((128, 128), np.float32)
    for g in range(32):
        bmask[g * 4:(g + 1) * 4, g * 4:(g + 1) * 4] = 1.0
    corelt = np.zeros((8, 8), np.float32)
    corelt[:core_id, 0] = 1.0
    return {"c_ident": ident, "c_ltri": ltri, "c_bmask": bmask,
            "c_corelt": corelt}



def record_ln_tile(S, c, bufs, x_dram, extra_stage_readers=()):
    nb = len(bufs["xt"])
    xt = bufs["xt"][c % nb]
    xn = bufs["xn"][c % nb]
    stage = bufs["stage"][c % nb]
    stat = bufs["stat"]
    sqd = bufs["sqd"]
    ps_tr = bufs["ps_tr"]
    ident_bf = bufs["ident_bf"]
    tk_xt = ("xt", c % nb)
    tk_xn = ("xn", c % nb)
    tk_st = ("stage", c % nb)

    S.step("sync", lambda e: e.dma_start(xt[:], x_dram[c * 128:(c + 1) * 128, :]),
           reads=(("x_dram", c),), writes=(tk_xt,), dma=True)
    S.step("scalar", lambda e: e.activation(sqd[:], xt[:], AF.Square,
                                            accum_out=stat[:, 1:2]),
           reads=(tk_xt,), writes=("lnsqd", "lnssq"))
    S.step("vector", lambda e: e.reduce_sum(stat[:, 0:1], xt[:], axis=AX.X),
           reads=(tk_xt,), writes=("lnssum",))
    S.step("vector", lambda e: e.tensor_scalar_mul(stat[:, 2:3], stat[:, 0:1], 1.0 / D),
           reads=("lnssum",), writes=("lnmean",))
    S.step("vector", lambda e: e.tensor_mul(stat[:, 7:8], stat[:, 2:3], stat[:, 2:3]),
           reads=("lnmean",), writes=("lnmsq",))
    S.step("vector", lambda e: e.tensor_scalar(stat[:, 3:4], stat[:, 1:2], 1.0 / D,
                                               stat[:, 7:8], ALU.mult, ALU.subtract),
           reads=("lnssq", "lnmsq"), writes=("lnvar",))
    S.step("scalar", lambda e: e.activation(stat[:, 4:5], stat[:, 3:4], AF.Sqrt,
                                            bias=bufs["epsc"][:, 0:1]),
           reads=("lnvar", "lneps"), writes=("lnsd",))
    S.step("vector", lambda e: e.reciprocal(stat[:, 5:6], stat[:, 4:5]),
           reads=("lnsd",), writes=("lnrstd",))
    S.step("vector", lambda e: e.tensor_scalar(xn[:], xt[:], stat[:, 2:3],
                                               stat[:, 5:6], ALU.subtract, ALU.mult),
           reads=(tk_xt, "lnmean", "lnrstd"), writes=(tk_xn,))

    def tr(e):
        for g in range(DC):
            i = e.transpose(ps_tr[:, g * 128:(g + 1) * 128],
                            xn[:, g * 128:(g + 1) * 128], ident_bf[:])
        return i
    S.step("tensor", tr, reads=(tk_xn, "ident_bf"), writes=("ps_tr",))
    if bufs.get("evac_dsts") is not None:
        dsts, dtok = bufs["evac_dsts"](c)
        def ev(e, dsts=dsts):
            for g in range(DC):
                i = e.copy(dsts[g], ps_tr[:, g * 128:(g + 1) * 128])
            return i
        S.step("scalar", ev, reads=("ps_tr",) + tuple(bufs.get("evac_extra_reads", ())),
               writes=(dtok,))
        return dtok
    S.step("scalar", lambda e: e.copy(stage[:], ps_tr[:]),
           reads=("ps_tr",), writes=(tk_st,))
    return tk_st


def alloc_ln_bufs(nc, ctx, sbf, psf, nbuf=2):
    return {
        "xt": [sbf(f"ln_xt{i}", [128, D], F32) for i in range(nbuf)],
        "xn": [sbf(f"ln_xn{i}", [128, D], BF16) for i in range(nbuf)],
        "stage": [sbf(f"ln_st{i}", [128, D], BF16) for i in range(nbuf)],
        "sqd": sbf("ln_sqd", [128, D], F32),
        "stat": sbf("ln_stat", [128, 8], F32),
        "epsc": sbf("ln_eps", [128, 1], F32),
        "ps_tr": psf("ps_tr", [128, 1024], BF16),
        "ident_bf": sbf("ident_bf", [128, 128], BF16),
    }



def build_ffn(layer, split="all", serialize=False, debug=False):
    H = 2048 if layer == 0 else 4096
    NFB = H // 2048
    FCB = 16
    G = 1024
    NG = T // G
    GT = G // 128
    NFI = H // 128
    MWSCALE = 0.5 if layer == 0 else 0.0625
    ACTF = AF.Gelu if layer == 0 else AF.Relu
    CAPTHR = 10239.5
    do_p1 = split in ("all", "count")
    do_p2 = split in ("all", "main")
    cap = (layer == 0)

    nc = bass.Bass("TRN2", target_bir_lowering=False, debug=False)
    c_ident = nc.dram_tensor("c_ident", [128, 128], F32, kind="ExternalInput")
    if do_p1:
        x_in = nc.dram_tensor("x_in", [T, D], F32, kind="ExternalInput")
        wg = nc.dram_tensor("wg", [D, E], F32, kind="ExternalInput")
        if cap:
            rb = nc.dram_tensor("rb", [1, E], F32, kind="ExternalInput")
    if do_p2:
        if split == "main":
            x_in = nc.dram_tensor("x_in", [T, D], F32, kind="ExternalInput")
        w1 = nc.dram_tensor("w1", [E, D, H], F32, kind="ExternalInput")
        b1T = nc.dram_tensor("b1T", [128, NFI * E], F32, kind="ExternalInput")
        w2 = nc.dram_tensor("w2", [E, H, D], F32, kind="ExternalInput")
        b2 = nc.dram_tensor("b2", [E, D], F32, kind="ExternalInput")
        x_out = nc.dram_tensor("x_out", [T, D], F32, kind="ExternalOutput")
    if split == "main":
        xnT_hbm = nc.dram_tensor("xnT_in", [D, T], BF16, kind="ExternalInput")
        mw_in = nc.dram_tensor("mw_in", [128, NT * E], F32, kind="ExternalInput")
        sel_in = nc.dram_tensor("sel_in", [128, NT * E], F32, kind="ExternalInput")
        agcnt = nc.dram_tensor("agcnt", [8, E], F32, kind="ExternalInput")
        pref_in = nc.dram_tensor("pref_in", [8, NT + 1], F32, kind="ExternalInput")
        c_ltri = nc.dram_tensor("c_ltri", [128, 128], F32, kind="ExternalInput")
        c_corelt = nc.dram_tensor("c_corelt", [8, 8], F32, kind="ExternalInput")
        nthr_hbm = nc.dram_tensor("scr_nthr", [NT + 1, E], F32)
    elif split == "count":
        xnT_hbm = nc.dram_tensor("xnT_out", [D, T], BF16, kind="ExternalOutput")
        mw_out = nc.dram_tensor("mw_out", [128, NT * E], F32, kind="ExternalOutput")
        sel_out = nc.dram_tensor("sel_out", [128, NT * E], F32, kind="ExternalOutput")
        pref_out = nc.dram_tensor("pref_out", [8, NT + 1], F32, kind="ExternalOutput")
    else:
        xnT_hbm = nc.dram_tensor("scr_xnT", [D, T], BF16)
    if debug and do_p2:
        dbg_mw = nc.dram_tensor("dbg_mw", [128, NT * E], F32, kind="ExternalOutput")
        dbg_acc = nc.dram_tensor("dbg_acc", [128, 8 * D], F32, kind="ExternalOutput")

    with ExitStack() as ctx:
        sbf = lambda n, s, d: ctx.enter_context(nc.sbuf_tensor(n, s, d))
        psf = lambda n, s, d: ctx.enter_context(nc.psum_tensor(n, s, d))
        S = Sched(nc, ctx, serialize=serialize)

        ident_bf = sbf("ident_bf2", [128, 128], BF16)
        S.step("gpsimd", lambda e: e.dma_start(ident_bf[:], c_ident[:, :]),
               writes=("ident_bf",), dma=True)
        maskw = sbf("maskw", [128, NT * E], F32)
        maskw_bf = sbf("maskw_bf", [128, 8], BF16)
        maskwT = sbf("maskwT", [8, T], BF16)
        ps_g = psf("ps_g", [128, 512], F32)
        ps_mwt = psf("ps_mwt", [128, 1024], BF16)

        if do_p1:
            lnb = alloc_ln_bufs(nc, ctx, sbf, psf)
            lnb["ident_bf"] = ident_bf
            wg_sb = sbf("wg_sb", [128, DC * E], BF16)
            gsb = sbf("gsb", [128, 8], F32)
            m8 = sbf("m8", [128, 8], F32)
            srow = sbf("srow", [128, 4], F32)
            esb = sbf("esb", [128, 8], F32)
            selsb = sbf("selsb", [128, 8], F32)
            S.step("vector", lambda e: e.memset(lnb["epsc"][:], 1e-5),
                   writes=("lneps",))
            S.step("gpsimd", lambda e: e.dma_start(
                wg_sb[:].rearrange("p (g e) -> p g e", e=E),
                wg[:, :].rearrange("(g p) e -> p g e", p=128)),
                writes=("wg_sb",), dma=True)
            if cap:
                rb_sb = sbf("rb_sb", [1, E], F32)
                ones_r = sbf("ones_r", [1, 128], F32)
                sel_a = sbf("sel_a", [128, NT * E], F32)
                onescol = sbf("onescol", [128, 1], F32)
                cnt_sb = sbf("cnt_sb", [8, NT], F32)
                pref_sb = sbf("pref_sb", [8, NT + 1], F32)
                ps_cnt = psf("ps_cnt", [128, 512], F32)
                S.step("sync", lambda e: e.dma_start(rb_sb[:], rb[:, :]),
                       writes=("rb_sb",), dma=True)
                S.step("vector", lambda e: e.memset(ones_r[:], 1.0), writes=("ones_r",))
                S.step("vector", lambda e: e.memset(onescol[:], 1.0), writes=("onescol",))

            for c in range(NT):
                tk_st = record_ln_tile(S, c, lnb, x_in)
                stage = lnb["stage"][c % 2]
                dst = xnT_hbm[:, c * 128:(c + 1) * 128].rearrange("(g p) k -> p g k", p=128)
                S.step("sync", lambda e, dst=dst, stage=stage: e.dma_start(
                    dst, stage[:].rearrange("p (g k) -> p g k", k=128)),
                    reads=(tk_st,), writes=(("xnT", c),), dma=True, fam=f"xnT{c % 2}")

                def gmm(e, stage=stage):
                    for g in range(DC):
                        i = e.matmul(ps_g[:, 0:8], stage[:, g * 128:(g + 1) * 128],
                                     wg_sb[:, g * E:(g + 1) * E],
                                     start=(g == 0), stop=(g == DC - 1 and not cap))
                    if cap:
                        i = e.matmul(ps_g[:, 0:8], ones_r[:], rb_sb[:],
                                     start=False, stop=True)
                    return i
                rds = (tk_st, "wg_sb") + (("ones_r", "rb_sb") if cap else ())
                S.step("tensor", gmm, reads=rds, writes=("ps_g",))
                S.step("vector", lambda e: e.tensor_copy(gsb[:], ps_g[:, 0:8]),
                       reads=("ps_g",), writes=("gsb",))
                S.step("vector", lambda e: e.max(out=m8[:], in_=gsb[:]),
                       reads=("gsb",), writes=("m8",))
                S.step("vector", lambda e: e.tensor_scalar_mul(srow[:, 0:1], m8[:, 0:1], -1.0),
                       reads=("m8",), writes=("negm1",))
                S.step("scalar", lambda e: e.activation(esb[:], gsb[:], AF.Exp,
                                                        bias=srow[:, 0:1]),
                       reads=("gsb", "negm1"), writes=("esb",))
                S.step("vector", lambda e: e.tensor_scalar(selsb[:], gsb[:], m8[:, 1:2],
                                                           None, ALU.subtract),
                       reads=("gsb", "m8"), writes=("selsb",))
                S.step("scalar", lambda e: e.activation(selsb[:], selsb[:], AF.Sign),
                       reads=("selsb",), writes=("selsb",))
                S.step("vector", lambda e: e.tensor_scalar(selsb[:], selsb[:], 1.0, 1.0,
                                                           ALU.add, ALU.min),
                       reads=("selsb",), writes=("selsb",))
                S.step("vector", lambda e: e.tensor_mul(esb[:], esb[:], selsb[:]),
                       reads=("esb", "selsb"), writes=("esb",))
                S.step("vector", lambda e: e.reduce_sum(srow[:, 1:2], esb[:], axis=AX.X),
                       reads=("esb",), writes=("essum",))
                S.step("vector", lambda e: e.reciprocal(srow[:, 2:3], srow[:, 1:2]),
                       reads=("essum",), writes=("esrs",))
                S.step("vector", lambda e, c=c: e.tensor_scalar(
                    maskw[:, c * E:(c + 1) * E], esb[:], srow[:, 2:3], MWSCALE,
                    ALU.mult, ALU.mult),
                    reads=("esb", "esrs"), writes=(("maskw", c),))
                if cap:
                    S.step("vector", lambda e, c=c: e.tensor_copy(
                        sel_a[:, c * E:(c + 1) * E], selsb[:]),
                        reads=("selsb",), writes=(("sel", c),))
                    S.step("tensor", lambda e, c=c: e.matmul(
                        ps_cnt[0:8, c:c + 1], sel_a[:, c * E:(c + 1) * E], onescol[:],
                        start=(c == 0), stop=(c == NT - 1), skip_group_check=True),
                        reads=(("sel", c), "onescol"), writes=(("ps_cnt", c),))
                else:
                    S.step("scalar", lambda e, c=c: e.copy(
                        maskw_bf[:], maskw[:, c * E:(c + 1) * E]),
                        reads=(("maskw", c),), writes=("maskw_bf",))
                    S.step("tensor", lambda e: e.transpose(ps_mwt[0:8, 0:128], maskw_bf[:],
                                                           ident_bf[:]),
                           reads=("maskw_bf", "ident_bf"), writes=("ps_mwt",))
                    S.step("scalar", lambda e, c=c: e.copy(maskwT[0:8, c * 128:(c + 1) * 128],
                                                           ps_mwt[0:8, 0:128]),
                           reads=("ps_mwt",), writes=(("maskwT", c),))
            if cap:
                cnt_all = tuple(("ps_cnt", c) for c in range(NT))
                S.step("scalar", lambda e: e.copy(cnt_sb[:], ps_cnt[0:8, 0:NT]),
                       reads=cnt_all, writes=("cnt_sb",))
                S.step("vector", lambda e: e.memset(pref_sb[0:8, 0:1], 0.0),
                       writes=("pref0",))
                S.step("vector", lambda e: e.tensor_tensor_scan(
                    pref_sb[0:8, 1:NT + 1], onescol[0:8, 0:1].to_broadcast([8, NT]),
                    cnt_sb[0:8, :], 0.0, ALU.mult, ALU.add),
                    reads=("cnt_sb", "onescol"), writes=("pref_sb",))

        if split == "count":
            mw_all = tuple(("maskw", c) for c in range(NT))
            S.step("sync", lambda e: e.dma_start(mw_out[:, :], maskw[:]),
                   reads=mw_all, dma=True)
            S.step("sync", lambda e: e.dma_start(sel_out[:, :], sel_a[:]),
                   reads=tuple(("sel", c) for c in range(NT)), dma=True)
            S.step("sync", lambda e: e.dma_start(pref_out[:, :], pref_sb[:]),
                   reads=("pref_sb", "pref0"), dma=True)
            S.build_block(ctx)
            return nc

        if split == "main":
            sel_a = sbf("sel_a", [128, NT * E], F32)
            ones_r = sbf("ones_r", [1, 128], F32)
            pref_sb = sbf("pref_sb", [8, NT + 1], F32)
            ag_sb = sbf("ag_sb", [8, 8], F32)
            corelt_sb = sbf("corelt_sb", [8, 8], F32)
            cb_row = sbf("cb_row", [1, 8], F32)
            negcap = sbf("negcap", [1, 8], F32)
            nthr_sb = sbf("nthr_sb", [33, 8], F32)
            nstag = [sbf(f"nstag{i}", [1, 8], F32) for i in range(2)]
            keep_sb = sbf("keep_sb", [128, 8], F32)
            ltri_sb = sbf("ltri_sb", [128, 128], F32)
            ident_f32 = sbf("ident_f32", [128, 128], F32)
            ps_rk = psf("ps_rk", [128, 512], F32)
            S.step("vector", lambda e: e.memset(ones_r[:], 1.0), writes=("ones_r",))
            S.step("sync", lambda e: e.dma_start(maskw[:], mw_in[:, :]),
                   writes=tuple(("maskw", c) for c in range(NT)), dma=True)
            S.step("sync", lambda e: e.dma_start(sel_a[:], sel_in[:, :]),
                   writes=tuple(("sel", c) for c in range(NT)), dma=True)
            S.step("sync", lambda e: e.dma_start(pref_sb[:], pref_in[:, :]),
                   writes=("pref_sb", "pref0"), dma=True)
            S.step("sync", lambda e: e.dma_start(ag_sb[:], agcnt[:, :]),
                   writes=("ag_sb",), dma=True)
            S.step("sync", lambda e: e.dma_start(corelt_sb[:], c_corelt[:, :]),
                   writes=("corelt_sb",), dma=True)
            S.step("sync", lambda e: e.dma_start(ltri_sb[:], c_ltri[:, :]),
                   writes=("ltri_sb",), dma=True)
            S.step("sync", lambda e: e.dma_start(ident_f32[:], c_ident[:, :]),
                   writes=("ident_f32",), dma=True)
            S.step("tensor", lambda e: e.matmul(ps_g[0:1, 0:8], corelt_sb[0:8, 0:1],
                                                ag_sb[0:8, :], start=True, stop=True),
                   reads=("corelt_sb", "ag_sb"), writes=("ps_g",))
            S.step("scalar", lambda e: e.copy(cb_row[:], ps_g[0:1, 0:8]),
                   reads=("ps_g",), writes=("cb_row",))
            S.step("vector", lambda e: e.tensor_scalar(negcap[:], cb_row[:], 1.0,
                                                       -CAPTHR, ALU.mult, ALU.add),
                   reads=("cb_row",), writes=("negcap",))

            def nthr_mm(e):
                e.transpose(ps_rk[0:33, 0:8], pref_sb[0:8, 0:33], ident_f32[0:8, 0:8])
                return e.matmul(ps_rk[0:33, 0:8], ones_r[0:1, 0:33], negcap[:],
                                start=False, stop=True)
            S.step("tensor", nthr_mm, reads=("pref_sb", "pref0", "negcap", "ones_r",
                                             "ident_f32"), writes=("ps_rk",))
            S.step("scalar", lambda e: e.copy(nthr_sb[:], ps_rk[0:33, 0:8]),
                   reads=("ps_rk",), writes=("nthr_sb",))
            S.step("sync", lambda e: e.dma_start(nthr_hbm[0:33, :], nthr_sb[:]),
                   reads=("nthr_sb",), writes=("nthr_hbm",), dma=True)
            for c in range(NT):
                S.step("gpsimd", lambda e, c=c: e.dma_start(
                    nstag[c % 2][:], nthr_hbm[c:c + 1, :]),
                    reads=("nthr_hbm",), writes=(("nstag", c % 2),), dma=True)

                def rkmm(e, c=c):
                    e.matmul(ps_rk[:, 0:8], ltri_sb[:], sel_a[:, c * E:(c + 1) * E],
                             start=True, stop=False)
                    return e.matmul(ps_rk[:, 0:8], ones_r[:], nstag[c % 2][:],
                                    start=False, stop=True)
                S.step("tensor", rkmm, reads=(("sel", c), ("nstag", c % 2),
                                              "ltri_sb", "ones_r"), writes=("ps_rk",))
                S.step("scalar", lambda e: e.activation(keep_sb[:], ps_rk[:, 0:8],
                                                        AF.Sign),
                       reads=("ps_rk",), writes=("keep_sb",))
                S.step("vector", lambda e: e.tensor_scalar(keep_sb[:], keep_sb[:],
                                                           -1.0, 0.0, ALU.mult, ALU.max),
                       reads=("keep_sb",), writes=("keep_sb",))
                S.step("vector", lambda e, c=c: e.tensor_mul(
                    maskw[:, c * E:(c + 1) * E], maskw[:, c * E:(c + 1) * E],
                    keep_sb[:]),
                    reads=(("maskw", c), "keep_sb"), writes=(("maskw", c),))
                S.step("scalar", lambda e, c=c: e.copy(
                    maskw_bf[:], maskw[:, c * E:(c + 1) * E]),
                    reads=(("maskw", c),), writes=("maskw_bf",))
                S.step("tensor", lambda e: e.transpose(ps_mwt[0:8, 0:128], maskw_bf[:],
                                                       ident_bf[:]),
                       reads=("maskw_bf", "ident_bf"), writes=("ps_mwt",))
                S.step("scalar", lambda e, c=c: e.copy(
                    maskwT[0:8, c * 128:(c + 1) * 128], ps_mwt[0:8, 0:128]),
                    reads=("ps_mwt",), writes=(("maskwT", c),))
            xnT_tokens = tuple(("maskwT", c) for c in range(NT))

        b1T_sb = sbf("b1T_sb", [128, NFI * E], F32)
        b2_bf = sbf("b2_bf", [8, D], BF16)
        xnG = sbf("xnG", [128, DC * G], BF16)
        h_sb = sbf("h_sb", [128, FCB * G], BF16)
        w1_sb = sbf("w1_sb", [128, DC * FCB * 128], BF16)
        w2_sb = sbf("w2_sb", [128, FCB * D], BF16)
        acc = sbf("acc", [128, GT * D], F32)
        xres = [sbf(f"xres{i}", [128, D], F32) for i in range(2)]
        xo = [sbf(f"xo{i}", [128, D], F32) for i in range(2)]
        ps_h = [psf(f"ps_h{i}", [128, 512], F32) for i in range(2)]
        ps_y = [psf(f"ps_y{i}", [128, 512], F32) for i in range(2)]
        S.step("gpsimd", lambda e: e.dma_start(b2_bf[:], b2[:, :]),
               writes=("b2_bf",), dma=True)
        S.step("sync", lambda e: e.dma_start(b1T_sb[:], b1T[:, :]),
               writes=("b1T_sb",), dma=True)

        mwT_all = tuple(("maskwT", c) for c in range(NT))
        if debug:
            S.step("sync", lambda e: e.dma_start(dbg_mw[:, :], maskw[:]),
                   reads=tuple(("maskw", c) for c in range(NT)), dma=True)
        xnT_rd = (lambda c: ("xnT", c)) if split != "main" else (lambda c: "xnT_ext")
        if split == "main":
            S.step("vector", lambda e: e.engine_nop(), reads=(), writes=("xnT_ext",))

        for g4 in range(NG):
            src = xnT_hbm[:, g4 * G:(g4 + 1) * G].rearrange("(g p) t -> p g t", p=128)
            S.step("sync", lambda e, src=src: e.dma_start(
                xnG[:].rearrange("p (g t) -> p g t", t=G), src),
                reads=tuple(xnT_rd(c) for c in range(g4 * GT, (g4 + 1) * GT)),
                writes=("xnG",), dma=True)
            for g8 in range(GT):
                c = g4 * GT + g8
                for dt in range(2):
                    S.step("tensor", lambda e, c=c, dt=dt: e.matmul(
                        ps_y[dt][:], maskwT[0:8, c * 128:(c + 1) * 128],
                        b2_bf[0:8, dt * 512:(dt + 1) * 512], start=True, stop=True),
                        reads=mwT_all + ("b2_bf",), writes=(f"ps_y{dt}",))
                    S.step("scalar", lambda e, g8=g8, dt=dt: e.copy(
                        acc[:, g8 * D + dt * 512: g8 * D + (dt + 1) * 512], ps_y[dt][:]),
                        reads=(f"ps_y{dt}",), writes=(("acc", g8, dt),))
            for e_i in range(E):
                for fb in range(NFB):
                    src1 = w1[e_i, :, fb * 2048:(fb + 1) * 2048].rearrange(
                        "(dc p) (fc k) -> p dc fc k", p=128, k=128)
                    S.step("gpsimd", lambda e, src1=src1: e.dma_start(
                        w1_sb[:].rearrange("p (dc fc k) -> p dc fc k", fc=FCB, k=128),
                        src1),
                        writes=("w1_sb",), dma=True)
                    src2 = w2[e_i, fb * 2048:(fb + 1) * 2048, :].rearrange(
                        "(fc p) d -> p fc d", p=128)
                    S.step("gpsimd", lambda e, src2=src2: e.dma_start(
                        w2_sb[:].rearrange("p (fc d) -> p fc d", d=D), src2),
                        writes=("w2_sb",), dma=True)
                    for fc in range(FCB):
                        for tb in range(2):
                            def hmm(e, fc=fc, tb=tb):
                                for dc in range(DC):
                                    i = e.matmul(
                                        ps_h[tb][:],
                                        w1_sb[:, (dc * FCB + fc) * 128:
                                              (dc * FCB + fc + 1) * 128],
                                        xnG[:, dc * G + tb * 512: dc * G + (tb + 1) * 512],
                                        start=(dc == 0), stop=(dc == DC - 1))
                                return i
                            S.step("tensor", hmm, reads=("w1_sb", "xnG"),
                                   writes=(f"ps_h{tb}",))
                            bcol = (fb * FCB + fc) * E + e_i
                            S.step("scalar", lambda e, fc=fc, tb=tb, bcol=bcol:
                                   e.activation(
                                       h_sb[:, fc * G + tb * 512: fc * G + (tb + 1) * 512],
                                       ps_h[tb][:], ACTF,
                                       bias=b1T_sb[:, bcol:bcol + 1]),
                                   reads=(f"ps_h{tb}", "b1T_sb"),
                                   writes=(("h", fc, tb),))
                    h_all = tuple(("h", fc, tb) for fc in range(FCB) for tb in range(2))
                    for g8 in range(GT):
                        for dt in range(2):
                            def ymm(e, g8=g8, dt=dt):
                                for fc in range(FCB):
                                    i = e.matmul(
                                        ps_y[dt][:],
                                        h_sb[:, fc * G + g8 * 128: fc * G + g8 * 128 + 128],
                                        w2_sb[:, fc * D + dt * 512: fc * D + (dt + 1) * 512],
                                        start=(fc == 0), stop=(fc == FCB - 1))
                                return i
                            S.step("tensor", ymm, reads=h_all + ("w2_sb",),
                                   writes=(f"ps_y{dt}",))
                            mcol = (g4 * GT + g8) * E + e_i
                            S.step("vector", lambda e, g8=g8, dt=dt, mcol=mcol:
                                   e.scalar_tensor_tensor(
                                       acc[:, g8 * D + dt * 512: g8 * D + (dt + 1) * 512],
                                       ps_y[dt][:], maskw[:, mcol:mcol + 1],
                                       acc[:, g8 * D + dt * 512: g8 * D + (dt + 1) * 512],
                                       ALU.mult, ALU.add),
                                   reads=(f"ps_y{dt}", ("maskw", g4 * GT + g8),
                                          ("acc", g8, dt)),
                                   writes=(("acc", g8, dt),))
            if debug and g4 == 0:
                S.step("sync", lambda e: e.dma_start(dbg_acc[:, :], acc[:]),
                       reads=tuple(("acc", g8, dt) for g8 in range(GT) for dt in range(2)),
                       dma=True)
            for g8 in range(GT):
                c = g4 * GT + g8
                S.step("sync", lambda e, c=c: e.dma_start(
                    xres[c % 2][:], x_in[c * 128:(c + 1) * 128, :]),
                    reads=(("x_dram", c),), writes=(("xres", c % 2),), dma=True)
                S.step("vector", lambda e, c=c, g8=g8: e.tensor_add(
                    xo[c % 2][:], xres[c % 2][:], acc[:, g8 * D:(g8 + 1) * D]),
                    reads=(("xres", c % 2), ("acc", g8, 0), ("acc", g8, 1)),
                    writes=(("xo", c % 2),))
                S.step("sync", lambda e, c=c: e.dma_start(
                    x_out[c * 128:(c + 1) * 128, :], xo[c % 2][:]),
                    reads=(("xo", c % 2),), writes=(("xout", c),), dma=True, fam=f"xout{c % 2}")

        S.build_block(ctx)
    return nc




def build_l1a(serialize=False, debug=False):
    nc = bass.Bass("TRN2", target_bir_lowering=False, debug=False)
    x_in = nc.dram_tensor("x_in", [T, D], F32, kind="ExternalInput")
    wkT = nc.dram_tensor("wkT", [D, D], F32, kind="ExternalInput")
    wvT = nc.dram_tensor("wvT", [D, D], F32, kind="ExternalInput")
    wrT = nc.dram_tensor("wrT", [D, D], F32, kind="ExternalInput")
    woT = nc.dram_tensor("woT", [D, D], F32, kind="ExternalInput")
    tmk5 = nc.dram_tensor("tmk5", [128, DC], F32, kind="ExternalInput")
    tmv5 = nc.dram_tensor("tmv5", [128, DC], F32, kind="ExternalInput")
    tmr5 = nc.dram_tensor("tmr5", [128, DC], F32, kind="ExternalInput")
    dec5 = nc.dram_tensor("dec5", [128, DC], F32, kind="ExternalInput")
    fst5 = nc.dram_tensor("fst5", [128, DC], F32, kind="ExternalInput")
    c_ident = nc.dram_tensor("c_ident", [128, 128], F32, kind="ExternalInput")
    x_out = nc.dram_tensor("x_out", [T, D], F32, kind="ExternalOutput")
    if debug:
        dbg_kv = nc.dram_tensor("dbg_kv", [128, 512 * 6], F32, kind="ExternalOutput")
        dbg_ry = nc.dram_tensor("dbg_ry", [128, 512 * 2], BF16, kind="ExternalOutput")
        dbg_ya = nc.dram_tensor("dbg_ya", [128, DC * 512], BF16, kind="ExternalOutput")
        dbg_xr = nc.dram_tensor("dbg_xr", [128, 2 * D], F32, kind="ExternalOutput")

    TC = 512
    NCH = T // TC
    W5 = TC + 1

    with ExitStack() as ctx:
        sbf = lambda n, s, d: ctx.enter_context(nc.sbuf_tensor(n, s, d))
        psf = lambda n, s, d: ctx.enter_context(nc.psum_tensor(n, s, d))
        S = Sched(nc, ctx, serialize=serialize)

        lnb = alloc_ln_bufs(nc, ctx, sbf, psf)
        S.step("gpsimd", lambda e: e.dma_start(lnb["ident_bf"][:], c_ident[:, :]),
               writes=("ident_bf",), dma=True)
        xnTa = sbf("xnTa", [128, DC * W5], BF16)

        def evac_dsts(c):
            ti = c % 4
            dsts = [xnTa[:, g * W5 + 1 + ti * 128: g * W5 + 1 + (ti + 1) * 128]
                    for g in range(DC)]
            return dsts, ("xnTa", c // 4)
        lnb["evac_dsts"] = evac_dsts
        lnb["evac_extra_reads"] = ("halo",)

        wT_sb = {}
        for nm, src in (("k", wkT), ("v", wvT), ("r", wrT), ("o", woT)):
            wT_sb[nm] = sbf(f"w{nm}T_sb", [128, DC * D], BF16)
            S.step("gpsimd", lambda e, nm=nm, src=src: e.dma_start(
                wT_sb[nm][:].rearrange("p (g e) -> p g e", e=D),
                src[:, :].rearrange("(g p) e -> p g e", p=128)),
                writes=(f"w{nm}T",), dma=True)
        tm_sb = {}
        for nm, src in (("k", tmk5), ("v", tmv5), ("r", tmr5), ("d", dec5), ("u", fst5)):
            tm_sb[nm] = sbf(f"tm{nm}_sb", [128, DC], F32)
            S.step("sync", lambda e, nm=nm, src=src: e.dma_start(tm_sb[nm][:], src[:, :]),
                   writes=(f"tm{nm}",), dma=True)
        ew = sbf("ew", [128, DC], F32)
        S.step("scalar", lambda e: e.activation(ew[:], tm_sb["d"][:], AF.Exp),
               reads=("tmd",), writes=("ew",))
        S.step("vector", lambda e: e.memset(lnb["epsc"][:], 1e-5), writes=("lneps",))

        Ast = sbf("Ast", [128, DC], F32)
        Pst = sbf("Pst", [128, DC], F32)
        S.step("vector", lambda e: e.memset(Ast[:], 0.0), writes=("Ast",))
        S.step("vector", lambda e: e.memset(Pst[:], 0.0), writes=("Pst",))
        S.step("vector", lambda e: e.memset(
            xnTa[:].rearrange("p (g c) -> p g c", c=W5)[:, :, 0:1], 0.0),
            writes=("halo",))

        dbuf = sbf("dbuf", [128, TC], BF16)
        xk_a = sbf("xk_a", [128, DC * TC], BF16)
        xv_a = sbf("xv_a", [128, DC * TC], BF16)
        xr_a = sbf("xr_a", [128, DC * TC], BF16)
        E1 = sbf("E1", [128, TC], F32)
        K1 = sbf("K1", [128, TC], F32)
        EV = sbf("EV", [128, TC], F32)
        KV = sbf("KV", [128, TC], F32)
        Abuf = sbf("Abuf", [128, W5], F32)
        Pbuf = sbf("Pbuf", [128, W5], F32)
        dnb = sbf("dnb", [128, TC], F32)
        recb = sbf("recb", [128, TC], F32)
        wkvb = sbf("wkvb", [128, TC], F32)
        r_a = sbf("r_a", [128, DC * TC], BF16)
        y_a = sbf("y_a", [128, DC * TC], BF16)
        xres = [sbf(f"xres{i}", [128, D], F32) for i in range(2)]
        xo = [sbf(f"xo{i}", [128, D], F32) for i in range(2)]

        ps_k = psf("ps_k", [128, 512], F32)
        ps_v = psf("ps_v", [128, 512], F32)
        ps_r = psf("ps_r", [128, 512], F32)
        ps_o = [psf(f"ps_o{i}", [128, 512], F32) for i in range(2)]

        for q in range(NCH):
            for ti in range(4):
                record_ln_tile(S, q * 4 + ti, lnb, x_in)
            tkx = ("xnTa", q)
            for dc in range(DC):
                sl = xnTa[:, dc * W5: (dc + 1) * W5]
                S.step("vector", lambda e, sl=sl: e.tensor_sub(
                    dbuf[:], sl[:, 1:W5], sl[:, 0:TC]),
                    reads=(tkx, "halo"), writes=("dbuf",))
                for nm, arr in (("k", xk_a), ("v", xv_a), ("r", xr_a)):
                    S.step("vector", lambda e, sl=sl, nm=nm, arr=arr, dc=dc:
                           e.scalar_tensor_tensor(
                               arr[:, dc * TC:(dc + 1) * TC], dbuf[:],
                               tm_sb[nm][:, dc:dc + 1], sl[:, 0:TC],
                               ALU.mult, ALU.add),
                           reads=("dbuf", f"tm{nm}", tkx, "halo"),
                           writes=((f"x{nm}", dc),))
            if q < NCH - 1:
                def haloc(e):
                    for dc in range(DC):
                        i = e.tensor_copy(xnTa[:, dc * W5: dc * W5 + 1],
                                          xnTa[:, dc * W5 + TC: dc * W5 + TC + 1])
                    return i
                S.step("vector", haloc, reads=(tkx,), writes=("halo",))
            xk_all = tuple(("xk", dc) for dc in range(DC))
            xv_all = tuple(("xv", dc) for dc in range(DC))
            xr_all = tuple(("xr", dc) for dc in range(DC))
            for ec in range(DC):
                def kmm(e, ec=ec, arr=xk_a, w="k", ps=ps_k):
                    for dc in range(DC):
                        i = e.matmul(ps[:], wT_sb[w][:, dc * D + ec * 128:
                                                     dc * D + (ec + 1) * 128],
                                     arr[:, dc * TC:(dc + 1) * TC],
                                     start=(dc == 0), stop=(dc == DC - 1))
                    return i
                S.step("tensor", lambda e, ec=ec: kmm(e, ec, xk_a, "k", ps_k),
                       reads=xk_all + ("wkT",), writes=("ps_k",))
                S.step("tensor", lambda e, ec=ec: kmm(e, ec, xv_a, "v", ps_v),
                       reads=xv_all + ("wvT",), writes=("ps_v",))
                S.step("tensor", lambda e, ec=ec: kmm(e, ec, xr_a, "r", ps_r),
                       reads=xr_all + ("wrT",), writes=("ps_r",))
                S.step("scalar", lambda e, ec=ec: e.activation(
                    E1[:], ps_k[:], AF.Exp, bias=tm_sb["u"][:, ec:ec + 1]),
                    reads=("ps_k", "tmu"), writes=("E1",))
                S.step("scalar", lambda e: e.activation(K1[:], ps_k[:], AF.Exp),
                       reads=("ps_k",), writes=("K1",))
                S.step("vector", lambda e: e.tensor_mul(EV[:], E1[:], ps_v[:]),
                       reads=("E1", "ps_v"), writes=("EV",))
                S.step("vector", lambda e: e.tensor_mul(KV[:], K1[:], ps_v[:]),
                       reads=("K1", "ps_v"), writes=("KV",))
                S.step("scalar", lambda e, ec=ec: e.activation(
                    r_a[:, ec * TC:(ec + 1) * TC], ps_r[:], AF.Sigmoid),
                    reads=("ps_r",), writes=(("r", ec),))
                S.step("vector", lambda e, ec=ec: e.tensor_copy(
                    Abuf[:, 0:1], Ast[:, ec:ec + 1]), reads=("Ast",), writes=("Abuf0",))
                S.step("vector", lambda e, ec=ec: e.tensor_tensor_scan(
                    Abuf[:, 1:W5], ew[:, ec:ec + 1].to_broadcast([128, TC]), KV[:],
                    Ast[:, ec:ec + 1], ALU.mult, ALU.add),
                    reads=("KV", "ew", "Ast"), writes=("Abuf",))
                S.step("vector", lambda e, ec=ec: e.tensor_copy(
                    Ast[:, ec:ec + 1], Abuf[:, TC:W5]),
                    reads=("Abuf",), writes=("Ast",))
                S.step("vector", lambda e, ec=ec: e.tensor_copy(
                    Pbuf[:, 0:1], Pst[:, ec:ec + 1]), reads=("Pst",), writes=("Pbuf0",))
                S.step("vector", lambda e, ec=ec: e.tensor_tensor_scan(
                    Pbuf[:, 1:W5], ew[:, ec:ec + 1].to_broadcast([128, TC]), K1[:],
                    Pst[:, ec:ec + 1], ALU.mult, ALU.max),
                    reads=("K1", "ew", "Pst"), writes=("Pbuf",))
                S.step("vector", lambda e, ec=ec: e.tensor_copy(
                    Pst[:, ec:ec + 1], Pbuf[:, TC:W5]),
                    reads=("Pbuf",), writes=("Pst",))
                S.step("vector", lambda e: e.tensor_add(dnb[:], Pbuf[:, 0:TC], E1[:]),
                       reads=("Pbuf", "Pbuf0", "E1"), writes=("dnb",))
                S.step("vector", lambda e: e.reciprocal(recb[:], dnb[:]),
                       reads=("dnb",), writes=("recb",))
                S.step("vector", lambda e: e.tensor_add(EV[:], Abuf[:, 0:TC], EV[:]),
                       reads=("Abuf", "Abuf0", "EV"), writes=("EV",))
                S.step("vector", lambda e: e.tensor_mul(wkvb[:], EV[:], recb[:]),
                       reads=("EV", "recb"), writes=("wkvb",))
                S.step("vector", lambda e, ec=ec: e.tensor_mul(
                    y_a[:, ec * TC:(ec + 1) * TC], wkvb[:],
                    r_a[:, ec * TC:(ec + 1) * TC]),
                    reads=("wkvb", ("r", ec)), writes=(("y", ec),))
                if debug and q == 0 and ec == 0:
                    S.step("gpsimd", lambda e: e.dma_start(
                        dbg_ry[:, 0:512], r_a[:, 0:512]), reads=(("r", 0),), dma=True)
                    S.step("gpsimd", lambda e: e.dma_start(
                        dbg_ry[:, 512:1024], y_a[:, 0:512]), reads=(("y", 0),), dma=True)
                    for j, (nm, t) in enumerate(
                            (("E1", E1), ("K1", K1), ("EV", EV), ("KV", KV),
                             ("wkv", wkvb), ("dn", dnb))):
                        S.step("sync", lambda e, j=j, t=t: e.dma_start(
                            dbg_kv[:, j * 512:(j + 1) * 512], t[:]),
                            reads=(nm if nm in ("E1", "K1", "KV", "dn") else
                                   ("EV" if nm == "EV" else "wkvb"),), dma=True)
            y_all = tuple(("y", ec) for ec in range(DC))
            if debug and q == 0:
                S.step("gpsimd", lambda e: e.dma_start(dbg_ya[:, :], y_a[:]),
                       reads=y_all, dma=True)
            for ti in range(4):
                c = q * 4 + ti
                S.step("sync", lambda e, c=c, ti=ti: e.dma_start(
                    xres[ti % 2][:], x_in[c * 128:(c + 1) * 128, :]),
                    reads=(("x_dram", c),), writes=(("xres", ti % 2),), dma=True)
                for eh in range(2):
                    def omm(e, ti=ti, eh=eh):
                        for dc in range(DC):
                            i = e.matmul(ps_o[eh][:],
                                         y_a[:, dc * TC + ti * 128: dc * TC + (ti + 1) * 128],
                                         wT_sb["o"][:, dc * D + eh * 512:
                                                    dc * D + (eh + 1) * 512],
                                         start=(dc == 0), stop=(dc == DC - 1))
                        return i
                    S.step("tensor", omm, reads=y_all + ("woT",),
                           writes=(f"ps_o{eh}",))
                    S.step("vector", lambda e, ti=ti, eh=eh: e.tensor_add(
                        xo[ti % 2][:, eh * 512:(eh + 1) * 512],
                        xres[ti % 2][:, eh * 512:(eh + 1) * 512], ps_o[eh][:]),
                        reads=(f"ps_o{eh}", ("xres", ti % 2)),
                        writes=(("xo", ti % 2, eh),))
                S.step("sync", lambda e, c=c, ti=ti: e.dma_start(
                    x_out[c * 128:(c + 1) * 128, :], xo[ti % 2][:]),
                    reads=(("xo", ti % 2, 0), ("xo", ti % 2, 1)),
                    writes=(("xout", c),), dma=True, fam=f"xout{ti % 2}")
                if debug and q == 0 and ti == 0:
                    S.step("sync", lambda e: e.dma_start(dbg_xr[:, 0:D], xres[0][:]),
                           reads=(("xres", 0),), dma=True)
                    S.step("sync", lambda e: e.dma_start(dbg_xr[:, D:2*D], xo[0][:]),
                           reads=(("xo", 0, 0), ("xo", 0, 1)), dma=True)

        S.build_block(ctx)
    return nc


def _l1a_inputs(x_shard, inputs, core_id):
    cons = host_consts(core_id)
    def five(v):
        return np.ascontiguousarray(
            np.asarray(v, np.float32).reshape(DC, 128).T)
    return {
        "x_in": x_shard,
        "wkT": np.ascontiguousarray(np.asarray(inputs["L1_attn_W_k"], np.float32).T),
        "wvT": np.ascontiguousarray(np.asarray(inputs["L1_attn_W_v"], np.float32).T),
        "wrT": np.ascontiguousarray(np.asarray(inputs["L1_attn_W_r"], np.float32).T),
        "woT": np.ascontiguousarray(np.asarray(inputs["L1_attn_W_o"], np.float32).T),
        "tmk5": five(inputs["L1_attn_time_mix_k"]),
        "tmv5": five(inputs["L1_attn_time_mix_v"]),
        "tmr5": five(inputs["L1_attn_time_mix_r"]),
        "dec5": five(inputs["L1_attn_time_decay"]),
        "fst5": five(inputs["L1_attn_time_first"]),
        "c_ident": cons["c_ident"],
    }




def build_l0a(serialize=False, debug=False):
    nc = bass.Bass("TRN2", target_bir_lowering=False, debug=False)
    x_in = nc.dram_tensor("x_in", [T, D], F32, kind="ExternalInput")
    wqkv = nc.dram_tensor("wqkv", [D, 3 * D], F32, kind="ExternalInput")
    wout = nc.dram_tensor("wout", [D, D], F32, kind="ExternalInput")
    bout = nc.dram_tensor("bout", [1, D], F32, kind="ExternalInput")
    c_ident = nc.dram_tensor("c_ident", [128, 128], F32, kind="ExternalInput")
    c_bmask = nc.dram_tensor("c_bmask", [128, 128], F32, kind="ExternalInput")
    x_out = nc.dram_tensor("x_out", [T, D], F32, kind="ExternalOutput")
    xnT_hbm = nc.dram_tensor("scr_xnT", [D, T], BF16)
    yB_hbm = nc.dram_tensor("scr_yB", [T, D], F32)

    TC = 512
    NCH = T // TC
    WA = TC + 4
    NH = 8
    SCALE = D ** -0.5

    with ExitStack() as ctx:
        sbf = lambda n, s, d: ctx.enter_context(nc.sbuf_tensor(n, s, d))
        psf = lambda n, s, d: ctx.enter_context(nc.psum_tensor(n, s, d))
        S = Sched(nc, ctx, serialize=serialize)

        lnb = alloc_ln_bufs(nc, ctx, sbf, psf)
        S.step("gpsimd", lambda e: e.dma_start(lnb["ident_bf"][:], c_ident[:, :]),
               writes=("ident_bf",), dma=True)
        S.step("vector", lambda e: e.memset(lnb["epsc"][:], 1e-5), writes=("lneps",))

        wq_sb = sbf("wq_sb", [128, DC * 3 * D], BF16)
        wo_sb = sbf("wo_sb", [128, DC * D], BF16)
        bout_sb = sbf("bout_sb", [1, D], F32)
        ones_r = sbf("ones_r", [1, 128], F32)
        bm_sb = sbf("bm_sb", [128, 128], F32)
        S.step("gpsimd", lambda e: e.dma_start(
            wq_sb[:].rearrange("p (g f) -> p g f", f=3 * D),
            wqkv[:, :].rearrange("(g p) f -> p g f", p=128)),
            writes=("wq_sb",), dma=True)
        S.step("gpsimd", lambda e: e.dma_start(
            wo_sb[:].rearrange("p (g f) -> p g f", f=D),
            wout[:, :].rearrange("(g p) f -> p g f", p=128)),
            writes=("wo_sb",), dma=True)
        S.step("sync", lambda e: e.dma_start(bout_sb[:], bout[:, :]),
               writes=("bout_sb",), dma=True)
        S.step("sync", lambda e: e.dma_start(bm_sb[:], c_bmask[:, :]),
               writes=("bm_sb",), dma=True)
        S.step("vector", lambda e: e.memset(ones_r[:], 1.0), writes=("ones_r",))

        xnTa = sbf("xnTa", [128, DC * WA], BF16)
        qT = sbf("qT", [128, NH * WA], BF16)
        kT = sbf("kT", [128, NH * WA], BF16)
        vA = sbf("vA", [128, 4 * D], BF16)
        vB = sbf("vB", [128, 4 * D], BF16)
        Pf = sbf("Pf", [128, NH * 128], F32)
        nmax = sbf("nmax", [128, NH], F32)
        dsum = sbf("dsum", [128, NH], F32)
        rden = sbf("rden", [128, NH], F32)
        Pb = sbf("Pb", [128, NH * 128], BF16)
        PTb = sbf("PTb", [128, NH * 128], BF16)
        OT = sbf("OT", [128, NH * 128], BF16)
        ybst = sbf("ybst", [128, D], F32)
        xres = [sbf(f"xres{i}", [128, D], F32) for i in range(2)]
        xo = [sbf(f"xo{i}", [128, D], F32) for i in range(2)]
        yb3 = [sbf(f"yb3_{i}", [128, D], F32) for i in range(2)]

        ps_s = [psf(f"ps_s{i}", [128, 512], F32) for i in range(2)]
        ps_pt = psf("ps_pt", [128, 1024], BF16)
        ps_av = [psf(f"ps_av{i}", [128, 512], F32) for i in range(2)]
        ps_y = [psf(f"ps_y{i}", [128, 512], F32) for i in range(2)]

        for c in range(NT):
            tk_st = record_ln_tile(S, c, lnb, x_in)
            stage = lnb["stage"][c % 2]
            dst = xnT_hbm[:, c * 128:(c + 1) * 128].rearrange("(g p) k -> p g k", p=128)
            S.step("sync", lambda e, dst=dst, stage=stage: e.dma_start(
                dst, stage[:].rearrange("p (g k) -> p g k", k=128)),
                reads=(tk_st,), writes=(("xnT", c),), dma=True, fam=f"xnT{c % 2}")

        S.step("vector", lambda e: e.memset(ybst[:], 0.0), writes=("ybst",))
        S.step("sync", lambda e: e.dma_start(yB_hbm[0:2, :], ybst[0:2, :]),
               reads=("ybst",), writes=("yB",), dma=True)
        S.step("sync", lambda e: e.dma_start(yB_hbm[T - 2:T, :], ybst[0:2, :]),
               reads=("ybst",), writes=("yB",), dma=True)

        for q in range(NCH):
            t0 = q * TC
            w_load = TC + 4 if q < NCH - 1 else TC
            src = xnT_hbm[:, t0:t0 + w_load].rearrange("(g p) t -> p g t", p=128)
            dstv = xnTa[:].rearrange("p (g j) -> p g j", j=WA)[:, :, 0:w_load]
            S.step("sync", lambda e, src=src, dstv=dstv: e.dma_start(dstv, src),
                   reads=tuple(("xnT", c) for c in range(q * 4, min(NT, q * 4 + 5))),
                   writes=("xnTa",), dma=True)
            if q == NCH - 1:
                S.step("vector", lambda e: e.memset(
                    xnTa[:].rearrange("p (g j) -> p g j", j=WA)[:, :, TC:WA], 0.0),
                    reads=(), writes=("xnTa_pad",))
            else:
                S.step("vector", lambda e: e.engine_nop(), writes=("xnTa_pad",))

            for h in range(NH):
                for nm, arr, fof in (("q", qT, 0), ("k", kT, D)):
                    def qkmm(e, h=h, arr=arr, fof=fof, main=True):
                        lo, hi = (0, TC) if main else (TC, WA)
                        for dc in range(DC):
                            i = e.matmul(
                                ps_s[0][:, 0:hi - lo],
                                xnTa[:, dc * WA + lo: dc * WA + hi],
                                wq_sb[:, dc * 3 * D + fof + h * 128:
                                      dc * 3 * D + fof + (h + 1) * 128],
                                start=(dc == 0), stop=(dc == DC - 1))
                        return i
                    def qkmmT(e, h=h, fof=fof, lo=0, hi=TC):
                        for dc in range(DC):
                            i = e.matmul(
                                ps_s[0][:, 0:hi - lo],
                                wq_sb[:, dc * 3 * D + fof + h * 128:
                                      dc * 3 * D + fof + (h + 1) * 128],
                                xnTa[:, dc * WA + lo: dc * WA + hi],
                                start=(dc == 0), stop=(dc == DC - 1))
                        return i
                    S.step("tensor", lambda e, f=qkmmT: f(e),
                           reads=("xnTa", "xnTa_pad", "wq_sb"), writes=("ps_s0",))
                    S.step("scalar", lambda e, h=h, arr=arr: e.copy(
                        arr[:, h * WA: h * WA + TC], ps_s[0][:, 0:TC]),
                        reads=("ps_s0",), writes=((nm + "T", h),))
                    S.step("tensor", lambda e, f=qkmmT: f(e, lo=TC, hi=WA),
                           reads=("xnTa", "xnTa_pad", "wq_sb"), writes=("ps_s0",))
                    S.step("scalar", lambda e, h=h, arr=arr: e.copy(
                        arr[:, h * WA + TC: h * WA + WA], ps_s[0][:, 0:WA - TC]),
                        reads=("ps_s0",), writes=((nm + "T", h),))
            qk_all = tuple(("qT", h) for h in range(NH)) + tuple(("kT", h) for h in range(NH))

            for br, varr, off in (("A", vA, 0), ("B", vB, 2)):
                for ti in range(4):
                    for vh in range(2):
                        def vmm(e, ti=ti, vh=vh, off=off):
                            for dc in range(DC):
                                i = e.matmul(
                                    ps_av[vh][:],
                                    xnTa[:, dc * WA + off + ti * 128:
                                          dc * WA + off + (ti + 1) * 128],
                                    wq_sb[:, dc * 3 * D + 2 * D + vh * 512:
                                          dc * 3 * D + 2 * D + (vh + 1) * 512],
                                    start=(dc == 0), stop=(dc == DC - 1))
                            return i
                        S.step("tensor", vmm, reads=("xnTa", "xnTa_pad", "wq_sb"),
                               writes=(f"ps_av{vh}",))
                        S.step("scalar", lambda e, ti=ti, vh=vh, varr=varr: e.copy(
                            varr[:, ti * D + vh * 512: ti * D + (vh + 1) * 512],
                            ps_av[vh][:]),
                            reads=(f"ps_av{vh}",), writes=((f"v{br}", ti),))

            for br, varr, off in (("A", vA, 0), ("B", vB, 2)):
                for ti in range(4):
                    if br == "B" and q == NCH - 1 and ti == 3:
                        nrows = 124
                    else:
                        nrows = 128
                    for hb in range(2):
                        def smm(e, hb=hb, ti=ti, off=off):
                            for hh in range(4):
                                h = hb * 4 + hh
                                i = e.matmul(
                                    ps_s[hb][:, hh * 128:(hh + 1) * 128],
                                    qT[:, h * WA + off + ti * 128:
                                       h * WA + off + (ti + 1) * 128],
                                    kT[:, h * WA + off + ti * 128:
                                       h * WA + off + (ti + 1) * 128],
                                    start=(hh == 0), stop=(hh == 3))
                            return i
                        S.step("tensor", smm, reads=qk_all, writes=(f"ps_s{hb}",))
                        S.step("vector", lambda e, hb=hb: e.reduce_max(
                            nmax[:, hb * 4:(hb + 1) * 4],
                            ps_s[hb][:].rearrange("p (hh t) -> p hh t", t=128),
                            axis=AX.X, negate=True),
                            reads=(f"ps_s{hb}",), writes=(("nmax", hb),))
                        S.step("vector", lambda e, hb=hb: e.tensor_scalar_mul(
                            nmax[:, hb * 4:(hb + 1) * 4],
                            nmax[:, hb * 4:(hb + 1) * 4], SCALE),
                            reads=(("nmax", hb),), writes=(("nmax", hb),))
                        for hh in range(4):
                            h = hb * 4 + hh
                            S.step("scalar", lambda e, hb=hb, hh=hh, h=h: e.activation(
                                Pf[:, h * 128:(h + 1) * 128],
                                ps_s[hb][:, hh * 128:(hh + 1) * 128], AF.Exp,
                                bias=nmax[:, h:h + 1], scale=SCALE),
                                reads=(f"ps_s{hb}", ("nmax", hb)), writes=(("Pf", h),))
                    for h in range(NH):
                        S.step("vector", lambda e, h=h: e.tensor_mul(
                            Pf[:, h * 128:(h + 1) * 128], Pf[:, h * 128:(h + 1) * 128],
                            bm_sb[:]),
                            reads=(("Pf", h), "bm_sb"), writes=(("Pf", h),))
                    S.step("vector", lambda e: e.reduce_sum(
                        dsum[:], Pf[:].rearrange("p (h t) -> p h t", t=128), axis=AX.X),
                        reads=tuple(("Pf", h) for h in range(NH)), writes=("dsum",))
                    S.step("vector", lambda e: e.reciprocal(rden[:], dsum[:]),
                           reads=("dsum",), writes=("rden",))
                    for h in range(NH):
                        S.step("vector", lambda e, h=h: e.tensor_scalar_mul(
                            Pb[:, h * 128:(h + 1) * 128], Pf[:, h * 128:(h + 1) * 128],
                            rden[:, h:h + 1]),
                            reads=(("Pf", h), "rden"), writes=(("Pb", h),))
                        S.step("tensor", lambda e, h=h: e.transpose(
                            ps_pt[:, h * 128:(h + 1) * 128],
                            Pb[:, h * 128:(h + 1) * 128], lnb["ident_bf"][:]),
                            reads=(("Pb", h), "ident_bf"), writes=("ps_pt",))
                        S.step("scalar", lambda e, h=h: e.copy(
                            PTb[:, h * 128:(h + 1) * 128],
                            ps_pt[:, h * 128:(h + 1) * 128]),
                            reads=("ps_pt",), writes=(("PTb", h),))
                    for hb in range(2):
                        def avmm(e, hb=hb, ti=ti, varr=varr):
                            for hh in range(4):
                                h = hb * 4 + hh
                                i = e.matmul(
                                    ps_av[hb][:, hh * 128:(hh + 1) * 128],
                                    varr[:, ti * D + h * 128: ti * D + (h + 1) * 128],
                                    PTb[:, h * 128:(h + 1) * 128],
                                    start=(hh == 0), stop=(hh == 3))
                            return i
                        S.step("tensor", avmm,
                               reads=tuple(("PTb", hb * 4 + hh) for hh in range(4))
                               + ((f"v{br}", ti),),
                               writes=(f"ps_av{hb}",))
                        S.step("scalar", lambda e, hb=hb: e.copy(
                            OT[:, hb * 512:(hb + 1) * 512], ps_av[hb][:]),
                            reads=(f"ps_av{hb}",), writes=(("OT", hb),))
                    for eh in range(2):
                        def ymm(e, eh=eh):
                            for h in range(NH):
                                e.matmul(
                                    ps_y[eh][:], OT[:, h * 128:(h + 1) * 128],
                                    wo_sb[:, h * D + eh * 512: h * D + (eh + 1) * 512],
                                    start=(h == 0), stop=False)
                            return e.matmul(ps_y[eh][:], ones_r[:],
                                            bout_sb[:, eh * 512:(eh + 1) * 512],
                                            start=False, stop=True)
                        S.step("tensor", ymm,
                               reads=(("OT", 0), ("OT", 1), "wo_sb", "ones_r",
                                      "bout_sb"),
                               writes=(f"ps_y{eh}",))
                    c = q * 4 + ti
                    if br == "A":
                        S.step("sync", lambda e, c=c, ti=ti: e.dma_start(
                            xres[ti % 2][:], x_in[c * 128:(c + 1) * 128, :]),
                            reads=(("x_dram", c),), writes=(("xres", ti % 2),),
                            dma=True)
                        for eh in range(2):
                            S.step("vector", lambda e, ti=ti, eh=eh: e.tensor_add(
                                xo[ti % 2][:, eh * 512:(eh + 1) * 512],
                                xres[ti % 2][:, eh * 512:(eh + 1) * 512],
                                ps_y[eh][:]),
                                reads=(f"ps_y{eh}", ("xres", ti % 2)),
                                writes=(("xo", ti % 2, eh),))
                        S.step("sync", lambda e, c=c, ti=ti: e.dma_start(
                            x_out[c * 128:(c + 1) * 128, :], xo[ti % 2][:]),
                            reads=(("xo", ti % 2, 0), ("xo", ti % 2, 1)),
                            writes=(("xout1", c),), dma=True, fam=f"xout{ti % 2}")
                    else:
                        for eh in range(2):
                            S.step("scalar", lambda e, eh=eh: e.copy(
                                ybst[:, eh * 512:(eh + 1) * 512], ps_y[eh][:]),
                                reads=(f"ps_y{eh}",), writes=("ybst",))
                        r0 = t0 + 2 + ti * 128
                        S.step("sync", lambda e, r0=r0, nrows=nrows: e.dma_start(
                            yB_hbm[r0:r0 + nrows, :], ybst[0:nrows, :]),
                            reads=("ybst",), writes=("yB",), dma=True)

        for c in range(NT):
            S.step("sync", lambda e, c=c: e.dma_start(
                xres[c % 2][:], x_out[c * 128:(c + 1) * 128, :]),
                reads=(("xout1", c),), writes=(("xres", c % 2),), dma=True)
            S.step("sync", lambda e, c=c: e.dma_start(
                yb3[c % 2][:], yB_hbm[c * 128:(c + 1) * 128, :]),
                reads=("yB",), writes=(("yb3", c % 2),), dma=True)
            S.step("vector", lambda e, c=c: e.tensor_add(
                xo[c % 2][:], xres[c % 2][:], yb3[c % 2][:]),
                reads=(("xres", c % 2), ("yb3", c % 2)), writes=(("xo", c % 2, 0),))
            S.step("sync", lambda e, c=c: e.dma_start(
                x_out[c * 128:(c + 1) * 128, :], xo[c % 2][:]),
                reads=(("xo", c % 2, 0),), writes=(("xout2", c),), dma=True, fam=f"xoutP3{c % 2}")

        S.build_block(ctx)
    return nc


def _l0a_inputs(x_shard, inputs, core_id):
    cons = host_consts(core_id)
    return {
        "x_in": x_shard,
        "wqkv": np.ascontiguousarray(inputs["L0_attn_w_qkv"], np.float32),
        "wout": np.ascontiguousarray(inputs["L0_attn_w_out"], np.float32),
        "bout": np.ascontiguousarray(
            np.asarray(inputs["L0_attn_b_out"], np.float32)).reshape(1, D),
        "c_ident": cons["c_ident"],
        "c_bmask": cons["c_bmask"],
    }



def _b1T(b1):
    Hh = b1.shape[1]
    return np.ascontiguousarray(
        b1.reshape(E, Hh // 128, 128).transpose(2, 1, 0).reshape(128, (Hh // 128) * E))


def _ffn_inputs(layer, x_shard, inputs, core_id, split="all", count_res=None,
                agcnt=None):
    cons = host_consts(core_id)
    if layer == 1:
        return {
            "x_in": x_shard,
            "wg": np.ascontiguousarray(inputs["L1_ffn_W_gate"], np.float32),
            "w1": inputs["L1_ffn_W1"],
            "b1T": _b1T(np.asarray(inputs["L1_ffn_b1"], np.float32)),
            "w2": inputs["L1_ffn_W2"],
            "b2": np.ascontiguousarray(inputs["L1_ffn_b2"], np.float32),
            "c_ident": cons["c_ident"],
        }
    if split == "count":
        return {
            "x_in": x_shard,
            "wg": np.ascontiguousarray(inputs["L0_ffn_router_w"], np.float32),
            "rb": np.ascontiguousarray(
                np.asarray(inputs["L0_ffn_router_b"], np.float32)).reshape(1, E),
            "c_ident": cons["c_ident"],
        }
    return {
        "x_in": x_shard,
        "w1": inputs["L0_ffn_w1"],
        "b1T": _b1T(np.asarray(inputs["L0_ffn_b1"], np.float32)),
        "w2": inputs["L0_ffn_w2"],
        "b2": np.ascontiguousarray(inputs["L0_ffn_b2"], np.float32),
        "c_ident": cons["c_ident"],
        "c_ltri": cons["c_ltri"],
        "c_corelt": cons["c_corelt"],
        "xnT_in": count_res["xnT_out"],
        "mw_in": count_res["mw_out"],
        "sel_in": count_res["sel_out"],
        "pref_in": count_res["pref_out"],
        "agcnt": agcnt,
    }


def run_l0f(x_shards, inputs, progs):
    nc_cnt, nc_main = progs
    cmaps = [_ffn_inputs(0, x_shards[b], inputs, b, split="count")
             for b in range(NCORES)]
    cres = run_bass_kernel_spmd(nc_cnt, cmaps, core_ids=list(range(NCORES))).results
    agcnt = np.ascontiguousarray(
        np.stack([cres[b]["pref_out"][:, NT] for b in range(NCORES)]))
    mmaps = [_ffn_inputs(0, x_shards[b], inputs, b, split="main",
                         count_res=cres[b], agcnt=agcnt) for b in range(NCORES)]
    mres = run_bass_kernel_spmd(nc_main, mmaps, core_ids=list(range(NCORES))).results
    return [mres[b]["x_out"] for b in range(NCORES)]


_PROGS = {}


def _get_progs():
    if not _PROGS:
        _PROGS["l0f_cnt"] = build_ffn(0, split="count")
        _PROGS["l0f_main"] = build_ffn(0, split="main")
        _PROGS["l0a"] = build_l0a()
        _PROGS["l1f"] = build_ffn(1)
        _PROGS["l1a"] = build_l1a()
    return _PROGS


def _run(prog, maps):
    res = run_bass_kernel_spmd(prog, maps, core_ids=list(range(NCORES))).results
    return [np.ascontiguousarray(np.asarray(res[b]["x_out"], np.float32))
            for b in range(NCORES)]


def kernel(**inputs):
    x = np.ascontiguousarray(np.asarray(inputs["L0_attn_x"], np.float32))
    P = _get_progs()
    sh = [np.ascontiguousarray(x[b]) for b in range(NCORES)]
    sh = run_l0f(sh, inputs, (P["l0f_cnt"], P["l0f_main"]))
    sh = _run(P["l0a"], [_l0a_inputs(sh[b], inputs, b) for b in range(NCORES)])
    sh = run_l0f(sh, inputs, (P["l0f_cnt"], P["l0f_main"]))
    sh = _run(P["l1f"], [_ffn_inputs(1, sh[b], inputs, b) for b in range(NCORES)])
    sh = _run(P["l1a"], [_l1a_inputs(sh[b], inputs, b) for b in range(NCORES)])
    sh = _run(P["l1f"], [_ffn_inputs(1, sh[b], inputs, b) for b in range(NCORES)])
    return np.stack(sh).astype(np.float32)
